# version 17
# baseline (speedup 1.0000x reference)
import contextlib
import numpy as np
import ml_dtypes
import concourse.bass as bass
import concourse.mybir as mybir
from concourse.bass_utils import run_bass_kernel_spmd
from concourse.alu_op_type import AluOpType

F32 = mybir.dt.float32
BF16 = mybir.dt.bfloat16
FP8 = mybir.dt.float8e4
AF = mybir.ActivationFunctionType
OP = AluOpType
AX = mybir.AxisListType.X

D = 1024
INNER = 4096
NJ = D // 128
NI = INNER // 128
MAGIC = 12582912.0
EPS = 1e-5


def build(L=8192, TT=512, dbg=False):
    NTT = L // TT
    NC = TT // 128
    NCH = NTT * NC
    PRV = 4
    PCH = 64
    CW = 512
    NWCH = 3 * PCH
    nc = bass.Bass("TRN2", target_bir_lowering=False,
                   detect_race_conditions=False)

    x_h = nc.dram_tensor("x", [L, D], F32, kind="ExternalInput").ap()
    wA_h = nc.dram_tensor("wA", [D, D], F32, kind="ExternalInput").ap()
    wSG_h = nc.dram_tensor("wSG", [D, D], F32, kind="ExternalInput").ap()
    wX_h = nc.dram_tensor("wX", [D, D], F32, kind="ExternalInput").ap()
    wG_h = nc.dram_tensor("wG", [D, D], F32, kind="ExternalInput").ap()
    wO_h = nc.dram_tensor("wO", [D, D], F32, kind="ExternalInput").ap()
    rmsT_h = nc.dram_tensor("rmsT", [128, D], F32, kind="ExternalInput").ap()
    bA_h = nc.dram_tensor("bAc", [128, NJ], F32, kind="ExternalInput").ap()
    wgT_h = nc.dram_tensor("wgT", [D, INNER], F32, kind="ExternalInput").ap()
    wuT_h = nc.dram_tensor("wuT", [D, INNER], F32, kind="ExternalInput").ap()
    wdT_h = nc.dram_tensor("wdT", [INNER, D], F32, kind="ExternalInput").ap()
    idb_h = nc.dram_tensor("idb", [128, 128], BF16, kind="ExternalInput").ap()
    ones1_h = nc.dram_tensor("ones1", [1, 128], F32, kind="ExternalInput").ap()
    onesc_h = nc.dram_tensor("onesc", [128, 1], F32, kind="ExternalInput").ap()
    out_h = nc.dram_tensor("out", [L, D], F32, kind="ExternalOutput").ap()
    x2s_h = nc.dram_tensor("x2s", [L, D], F32,
                           kind="ExternalOutput" if dbg else "Internal").ap()
    xqs_h = nc.dram_tensor("xqs", [L, D], BF16,
                           kind="ExternalOutput" if dbg else "Internal").ap()
    _wqk = "ExternalOutput" if dbg else "Internal"
    wq_h = [nc.dram_tensor("wq_g", [D, INNER], FP8, kind=_wqk).ap(),
            nc.dram_tensor("wq_u", [D, INNER], FP8, kind=_wqk).ap(),
            nc.dram_tensor("wq_d", [INNER, D], FP8, kind=_wqk).ap()]

    cx = contextlib.ExitStack()

    def mk(stack):
        def sbf(name, shape, dt):
            return stack.enter_context(nc.sbuf_tensor(name, shape, dt))
        def psf(name, shape, dt=F32):
            return stack.enter_context(nc.psum_tensor(name, shape, dt))
        return sbf, psf

    sb, _ps = mk(cx)

    def sem(name):
        return cx.enter_context(nc.semaphore(name))

    with cx:
        ident = sb("ident", [128, 128], BF16)
        ones1 = sb("ones1s", [1, 128], F32)
        onesc = sb("onescs", [128, 1], F32)
        s_all = sb("s_all", [128, NCH], F32)
        sg_all = sb("sg_all", [128, NCH], F32)
        su_all = sb("su_all", [128, NCH], F32)
        swcols = sb("swcols", [128, 8], F32)
        ld0 = sem("ld0")
        NLD0 = 10

        sm_xld = sem("s_xld"); sm_xfv = sem("s_xfv")
        sm_sq1 = sem("s_sq1"); sm_rqv = sem("s_rqv"); sm_rqs = sem("s_rqs")
        sm_rv1 = sem("s_rv1"); sm_rs1 = sem("s_rs1")
        sm_rv2 = sem("s_rv2"); sm_rs2 = sem("s_rs2")
        sm_xn = sem("s_xn"); sm_xtr = sem("s_xtr"); sm_xne = sem("s_xne")
        sm_gA = sem("s_gA"); sm_gSG = sem("s_gSG")
        sm_gX = sem("s_gX"); sm_gG = sem("s_gG")
        sm_eA = sem("s_eA"); sm_eSG = sem("s_eSG"); sm_eG = sem("s_eG")
        sm_u = sem("s_u"); sm_sc = sem("s_sc"); sm_cy = sem("s_cy")
        sm_y = sem("s_y"); sm_op = sem("s_op"); sm_x2e = sem("s_x2e")
        sm_x2t = sem("s_x2t"); sm_xqv = sem("s_xqv"); sm_xqt = sem("s_xqt")
        sm_sq2 = sem("s_sq2")
        sm_pld = sem("s_pld"); sm_pv = sem("s_pv"); sm_ps2 = sem("s_ps2")
        sm_pst = sem("s_pst"); sm_pmm = sem("s_pmm")
        sm_wq = sem("s_wq"); sm_qtl = sem("s_qtl"); sm_x2l = sem("s_x2l")
        sm_gu = sem("s_gu"); sm_sil = sem("s_sil"); sm_t1 = sem("s_t1")
        sm_q2 = sem("s_q2"); sm_q2t = sem("s_q2t"); sm_q2e = sem("s_q2e")
        sm_dn = sem("s_dn"); sm_dne = sem("s_dne"); sm_ost = sem("s_ost")
        sm_vf = sem("s_vf"); sm_sf = sem("s_sf")

        blk = cx.enter_context(nc.Block())

        def pv2(j):
            return NWCH + 1 + j + 1

        p1 = contextlib.ExitStack()
        sb1, ps1 = mk(p1)
        with p1:
            x_tm = sb1("x_tm", [128, NC * D], F32)
            pst = [sb1(f"pst{i}", [128, 512], F32) for i in range(4)]
            pz = [sb1(f"pz{i}", [128, 512], F32) for i in range(3)]
            pq8 = [sb1(f"pq8{i}", [128, 512], FP8) for i in range(2)]
            mcol = sb1("mcol", [128, 192], F32)
            sspc = sb1("sspc", [128, 8], F32)
            prow = sb1("prow", [1, 192], F32)
            wscr = sb1("wscr", [128, 8], F32)
            xn_tm = sb1("xn_tm", [128, NC * D], BF16)
            r_tm = sb1("r_tm", [128, NC * D], F32)
            xnF = [sb1(f"xnF{i}", [128, NJ * TT], BF16) for i in range(2)]
            wP = [sb1(f"wP{i}", [128, NJ * D], BF16) for i in range(4)]
            wo_s = sb1("wo_s", [128, NJ * D], BF16)
            rms_t = sb1("rms_t", [128, D], F32)
            bA_c = sb1("bA_c", [128, NJ], F32)
            A_sb = [sb1(f"A_sb{i}", [128, TT], F32) for i in range(2)]
            sg_sb = sb1("sg_sb", [128, TT], F32)
            g_sb = [sb1(f"g_sb{i}", [128, TT], F32) for i in range(2)]
            u_sb = sb1("u_sb", [128, TT], F32)
            h_sb = [sb1(f"h_sb{i}", [128, TT], F32) for i in range(2)]
            y_sb = sb1("y_sb", [128, NJ * TT], BF16)
            carry = sb1("carry", [128, NJ], F32)
            x2c = [sb1(f"x2c{i}", [128, D], F32) for i in range(4)]
            xqc = [sb1(f"xqc{i}", [128, D], BF16) for i in range(2)]
            st1 = sb1("st1", [128, 112], F32)
            st2 = sb1("st2", [128, 112], F32)
            tmpc = sb1("tmpc", [128, D], F32)
            ptr = [ps1(f"ptr{i}", [128, 512], BF16) for i in range(2)]
            ppj = [ps1(f"ppj{i}", [128, 512]) for i in range(4)]
            pop = [ps1(f"pop{i}", [128, 512]) for i in range(2)]

            WSRC = [wgT_h, wuT_h, wdT_h]

            def chunk(w, k, hbm):
                if w < 2:
                    r0, c0 = (k // 8) * 128, (k % 8) * CW
                else:
                    r0, c0 = (k // 2) * 128, (k % 2) * CW
                return hbm[r0:r0 + 128, c0:c0 + CW]

            def _prep_ld(s, j, rnd):
                w, k = j // PCH, j % PCH
                idx = rnd * NWCH + j
                if rnd == 0:
                    if idx >= 4:
                        s.wait_ge(sm_pv, idx - 3)
                else:
                    s.wait_ge(sm_pv, NWCH if j < 4 else pv2(j - 4))
                s.dma_start(out=pst[idx % 4][:],
                            in_=chunk(w, k, WSRC[w])).then_inc(sm_pld, 16)

            def _prep_v1(v, j):
                v.wait_ge(sm_pld, 16 * (j + 1))
                inst = v.tensor_reduce(out=mcol[:, j:j + 1], in_=pst[j % 4][:],
                                       axis=AX, op=OP.add,
                                       apply_absolute_value=True)
                if j < NWCH - 1:
                    inst.then_inc(sm_pv, 1)
                else:
                    for _ in range(4):
                        v.tensor_copy(out=wscr[:, 6:7], in_=mcol[:, 0:1])
                    v.tensor_copy(out=wscr[:, 7:8], in_=mcol[:, 0:1]
                                  ).then_inc(sm_pv, 1)

            def _prep_v2(v, j):
                idx = NWCH + j
                v.wait_ge(sm_pld, 16 * (idx + 1))
                if j >= 3:
                    v.wait_ge(sm_ps2, j - 2)
                w = j // PCH
                v.tensor_scalar(out=pz[j % 3][:], in0=pst[idx % 4][:],
                                scalar1=swcols[:, w:w + 1], scalar2=MAGIC,
                                op0=OP.mult, op1=OP.add)
                v.tensor_scalar(out=pz[j % 3][:], in0=pz[j % 3][:],
                                scalar1=MAGIC - 1.0, scalar2=MAGIC + 1.0,
                                op0=OP.max, op1=OP.min).then_inc(sm_pv, 1)

            def _prep_s2(s, j):
                w, k = j // PCH, j % PCH
                s.wait_ge(sm_pv, min(pv2(j) + 2, 2 * NWCH + 2))
                if j >= 2:
                    s.wait_ge(sm_pst, 16 * (j - 1))
                s.activation(out=pq8[j % 2][:], in_=pz[j % 3][:], func=AF.Copy,
                             bias=-MAGIC).then_inc(sm_ps2, 1)
                s.dma_start(out=chunk(w, k, wq_h[w]),
                            in_=pq8[j % 2][:]).then_inc(sm_pst, 16)

            def _prep_vcmb(v):
                v.wait_ge(sm_pmm, 3)
                for w in range(3):
                    v.tensor_reduce(out=wscr[:, w:w + 1],
                                    in_=ppj[1][:, w * 64:(w + 1) * 64],
                                    axis=AX, op=OP.add)
                v.tensor_copy(out=wscr[:, 6:7], in_=wscr[:, 0:1])
                v.tensor_copy(out=wscr[:, 7:8], in_=wscr[:, 0:1])
                v.tensor_scalar(out=wscr[:, 3:6], in0=wscr[:, 0:3],
                                scalar1=1.0 / (D * INNER), scalar2=1e-5,
                                op0=OP.mult, op1=OP.max)
                v.tensor_copy(out=wscr[:, 6:7], in_=wscr[:, 0:1])
                v.tensor_copy(out=wscr[:, 7:8], in_=wscr[:, 0:1])
                v.reciprocal(out=swcols[:, 0:3], in_=wscr[:, 3:6])
                v.tensor_scalar(out=swcols[:, 3:4], in0=wscr[:, 5:6],
                                scalar1=1.0 / 127.0, scalar2=None,
                                op0=OP.mult)
                for _ in range(4):
                    v.tensor_copy(out=wscr[:, 6:7], in_=mcol[:, 0:1])
                v.tensor_copy(out=wscr[:, 7:8], in_=mcol[:, 0:1]
                              ).then_inc(sm_pv, 1)

            def _prep_scales(v):
                v.tensor_scalar(out=sg_all[:], in0=s_all[:],
                                scalar1=swcols[:, 0:1], scalar2=None, op0=OP.mult)
                v.tensor_scalar(out=su_all[:], in0=s_all[:],
                                scalar1=swcols[:, 1:2], scalar2=None, op0=OP.mult)
                v.tensor_copy(out=wscr[:, 6:7], in_=wscr[:, 0:1])
                v.tensor_copy(out=wscr[:, 7:8], in_=wscr[:, 0:1])
                v.reciprocal(out=sg_all[:], in_=sg_all[:])
                v.reciprocal(out=su_all[:], in_=su_all[:]).then_inc(sm_pv, 1)

            CT = min(7, NTT - 2)
            P1N = -(-NWCH // ((CT + 1) * NJ))
            P2N = -(-NWCH // ((NTT - 1 - CT) * NJ))

            def q1(tt):
                if tt >= CT:
                    return NWCH
                return min(NWCH, P1N * NJ * (tt + 1))

            def q2q(tt):
                if tt <= CT:
                    return 0
                return min(NWCH, P2N * NJ * (tt - CT))



            @blk.gpsimd
            def _(g):
                def r3(t):
                    return t[:].rearrange("p (k n) -> p k n", k=NJ)
                for dst, src in [
                        (ident[:], idb_h), (ones1[:], ones1_h), (onesc[:], onesc_h),
                        (rms_t[:], rmsT_h), (bA_c[:], bA_h),
                        (r3(wP[0]), wA_h.rearrange("(k p) n -> p k n", p=128)),
                        (r3(wP[1]), wSG_h.rearrange("(k p) n -> p k n", p=128)),
                        (r3(wP[2]), wX_h.rearrange("(k p) n -> p k n", p=128)),
                        (r3(wP[3]), wG_h.rearrange("(k p) n -> p k n", p=128)),
                        (r3(wo_s), wO_h.rearrange("(k p) n -> p k n", p=128))]:
                    g.dma_start(out=dst, in_=src).then_inc(ld0, 16)
                for k in range(NTT * NC):
                    g.wait_ge(sm_xqv, k + 1)
                    g.dma_start(out=xqs_h[k * 128:(k + 1) * 128, :],
                                in_=xqc[k % 2][:]).then_inc(sm_xqt, 16)

            @blk.sync
            def _(s):
                xv = x_h.rearrange("(tt c p) d -> tt p c d", c=NC, p=128)
                ld_done = [0, 0]
                for tt in range(NTT):
                    if tt >= 1:
                        s.wait_ge(sm_xfv, tt)
                    s.dma_start(out=x_tm[:].rearrange("p (c d) -> p c d", c=NC), in_=xv[tt]).then_inc(sm_xld, 16)
                    while ld_done[0] < q1(tt):
                        _prep_ld(s, ld_done[0], 0); ld_done[0] += 1
                    while ld_done[1] < q2q(tt):
                        _prep_ld(s, ld_done[1], 1); ld_done[1] += 1
                    if tt >= 1:
                        for c in range(NC):
                            k = (tt - 1) * NC + c
                            s.wait_ge(sm_x2e, k + 1)
                            s.dma_start(out=x2s_h[k * 128:(k + 1) * 128, :],
                                        in_=x2c[c][:]).then_inc(sm_x2t, 16)
                for c in range(NC):
                    k = (NTT - 1) * NC + c
                    s.wait_ge(sm_x2e, k + 1)
                    s.dma_start(out=x2s_h[k * 128:(k + 1) * 128, :],
                                in_=x2c[c][:]).then_inc(sm_x2t, 16)

            @blk.tensor
            def _(t):
                t.wait_ge(ld0, 16 * NLD0)
                for tt in range(NTT):
                    tb = tt % 2
                    for c in range(NC):
                        t.wait_ge(sm_xn, tt * NC + c + 1)
                        for half in range(2):
                            bidx = tt * 8 + c * 2 + half
                            if bidx >= 2:
                                t.wait_ge(sm_xne, bidx - 1)
                            pb = ptr[bidx % 2]
                            for jj in range(4):
                                j = half * 4 + jj
                                inst = t.transpose(
                                    pb[:, jj * 128:(jj + 1) * 128],
                                    xn_tm[:, c * D + j * 128:c * D + (j + 1) * 128],
                                    ident[:])
                                if jj == 3:
                                    inst.then_inc(sm_xtr, 1)
                    t.wait_ge(sm_xne, tt * 8 + 8)
                    if tt == CT + 1:
                        t.wait_ge(sm_pmm, 2)
                        t.wait_ge(sm_eSG, (CT + 1) * NJ)
                        t.matmul(ppj[1][:, 0:192], ones1[:], prow[0:1, 0:192],
                                 start=True, stop=True).then_inc(sm_pmm, 1)
                        t.wait_ge(sm_pv, NWCH + 1)
                    for n in range(NJ):
                        k = tt * NJ + n
                        t.wait_ge(sm_eA, k)
                        for d in range(NJ):
                            inst = t.matmul(
                                ppj[0][:],
                                wP[0][:, d * D + n * 128:d * D + (n + 1) * 128],
                                xnF[tb][:, d * TT:(d + 1) * TT],
                                start=(d == 0), stop=(d == NJ - 1))
                            if d == NJ - 1:
                                inst.then_inc(sm_gA, 1)
                        t.wait_ge(sm_eSG, k)
                        for d in range(NJ):
                            inst = t.matmul(
                                ppj[1][:],
                                wP[1][:, d * D + n * 128:d * D + (n + 1) * 128],
                                xnF[tb][:, d * TT:(d + 1) * TT],
                                start=(d == 0), stop=(d == NJ - 1))
                            if d == NJ - 1:
                                inst.then_inc(sm_gSG, 1)
                        t.wait_ge(sm_u, k)
                        for d in range(NJ):
                            inst = t.matmul(
                                ppj[2][:],
                                wP[2][:, d * D + n * 128:d * D + (n + 1) * 128],
                                xnF[tb][:, d * TT:(d + 1) * TT],
                                start=(d == 0), stop=(d == NJ - 1))
                            if d == NJ - 1:
                                inst.then_inc(sm_gX, 1)
                        t.wait_ge(sm_eG, k)
                        for d in range(NJ):
                            inst = t.matmul(
                                ppj[3][:],
                                wP[3][:, d * D + n * 128:d * D + (n + 1) * 128],
                                xnF[tb][:, d * TT:(d + 1) * TT],
                                start=(d == 0), stop=(d == NJ - 1))
                            if d == NJ - 1:
                                inst.then_inc(sm_gG, 1)
                    t.wait_ge(sm_y, tt * NJ + NJ)
                    for c in range(NC):
                        k = tt * NC + c
                        if k >= 1:
                            t.wait_ge(sm_x2e, k)
                        blo, bhi = pop[0], pop[1]
                        for n in range(NJ):
                            t.matmul(blo[:],
                                     y_sb[:, n * TT + c * 128:n * TT + (c + 1) * 128],
                                     wo_s[:, n * D:n * D + 512],
                                     start=(n == 0), stop=(n == NJ - 1))
                            inst = t.matmul(
                                bhi[:],
                                y_sb[:, n * TT + c * 128:n * TT + (c + 1) * 128],
                                wo_s[:, n * D + 512:n * D + 1024],
                                start=(n == 0), stop=(n == NJ - 1))
                            if n == NJ - 1:
                                inst.then_inc(sm_op, 1)
                    if tt == CT:
                        t.wait_ge(sm_pv, NWCH)
                        t.wait_ge(sm_eA, (CT + 1) * NJ)
                        t.matmul(ppj[0][0:1, 0:192], onesc[:], mcol[:],
                                 start=True, stop=True).then_inc(sm_pmm, 1)

            def _bn_emit(v, st, c, xin):
                b0 = 36 + 12 * c
                v.bn_stats(out=st[:, b0:b0 + 6], in_=xin[:, 0:512])
                v.bn_stats(out=st[:, b0 + 6:b0 + 12], in_=xin[:, 512:1024])

            def _bn_aggr_all(v, st):
                for c in range(NC):
                    b0 = 36 + 12 * c
                    v.tensor_reduce(out=st[:, c:c + 1],
                                    in_=st[:, b0 + 1:b0 + 12:3], axis=AX, op=OP.add)
                for c in range(NC):
                    b0 = 36 + 12 * c
                    v.tensor_reduce(out=st[:, 100 + c:101 + c],
                                    in_=st[:, b0 + 2:b0 + 12:3], axis=AX, op=OP.add)
                for c in range(NC):
                    b0 = 36 + 12 * c
                    v.tensor_tensor(out=st[:, 84 + 4 * c:88 + 4 * c],
                                    in0=st[:, b0 + 1:b0 + 12:3],
                                    in1=st[:, b0 + 1:b0 + 12:3], op=OP.mult)
                for c in range(NC):
                    v.tensor_reduce(out=st[:, 104 + c:105 + c],
                                    in_=st[:, 84 + 4 * c:88 + 4 * c],
                                    axis=AX, op=OP.add)
                for c in range(NC):
                    v.tensor_scalar(out=st[:, c:c + 1], in0=st[:, c:c + 1],
                                    scalar1=0.25, scalar2=None, op0=OP.mult)
                for c in range(NC):
                    v.scalar_tensor_tensor(out=st[:, 100 + c:101 + c],
                                           in0=st[:, 104 + c:105 + c],
                                           scalar=float(D // 4),
                                           in1=st[:, 100 + c:101 + c],
                                           op0=OP.mult, op1=OP.add)
                for c in range(NC):
                    v.tensor_tensor(out=st[:, 108 + c:109 + c], in0=st[:, c:c + 1],
                                    in1=st[:, c:c + 1], op=OP.mult)
                for c in range(NC):
                    v.scalar_tensor_tensor(out=st[:, 4 + c:5 + c],
                                           in0=st[:, 100 + c:101 + c],
                                           scalar=1.0 / D,
                                           in1=st[:, 108 + c:109 + c],
                                           op0=OP.mult, op1=OP.subtract)

            def _v_head(v, tt, ev):
                tb = tt % 2
                v.wait_ge(sm_xld, 16 * (tt + 1))
                if tt >= 1:
                    v.wait_ge(sm_xtr, (tt - 1) * 8 + 8)
                for c in range(NC):
                    _bn_emit(v, st1, c, x_tm[:, c * D:(c + 1) * D])
                    if ev is not None:
                        _v_evac1(v, ev, c)
                v.memset(st1[:, 88:92], 0.0).then_inc(sm_rv1, 1)
                v.wait_ge(sm_rs1, 2 * tt + 1)
                _bn_aggr_all(v, st1)
                for c in range(NC):
                    v.tensor_scalar(out=st1[:, 8 + c:9 + c],
                                    in0=st1[:, 4 + c:5 + c], scalar1=EPS,
                                    scalar2=None, op0=OP.add)
                for c in range(NC):
                    v.reciprocal(out=st1[:, 28 + c:29 + c], in_=st1[:, 8 + c:9 + c])
                for c in range(NC):
                    v.tensor_tensor(out=st1[:, 32 + c:33 + c],
                                    in0=st1[:, 4 + c:5 + c],
                                    in1=st1[:, 28 + c:29 + c], op=OP.mult)
                for c in range(NC):
                    inst = v.tensor_scalar(out=st1[:, 12 + c:13 + c],
                                           in0=st1[:, 32 + c:33 + c],
                                           scalar1=EPS, scalar2=None, op0=OP.add)
                    if c == NC - 1:
                        inst.then_inc(sm_rv1, 1)
                v.wait_ge(sm_rs1, 2 * tt + 2)
                v.tensor_tensor(out=st1[:, 24:28], in0=st1[:, 16:20],
                                in1=st1[:, 20:24], op=OP.mult)
                v.tensor_copy(out=st1[:, 92:96], in_=st1[:, 88:92])
                v.tensor_copy(out=st1[:, 88:92], in_=st1[:, 92:96])
                v.reciprocal(out=st1[:, 24:28], in_=st1[:, 24:28])
                v.reciprocal(out=st1[:, 16:20], in_=st1[:, 16:20])
                v.tensor_copy(out=st1[:, 92:96], in_=st1[:, 88:92])
                v.tensor_copy(out=st1[:, 88:92], in_=st1[:, 92:96])
                for c in range(NC):
                    v.tensor_scalar(out=tmpc[:], in0=x_tm[:, c * D:(c + 1) * D],
                                    scalar1=st1[:, c:c + 1],
                                    scalar2=st1[:, 24 + c:25 + c],
                                    op0=OP.subtract, op1=OP.mult)
                    v.tensor_tensor(out=xn_tm[:, c * D:(c + 1) * D],
                                    in0=tmpc[:], in1=rms_t[:],
                                    op=OP.mult).then_inc(sm_xn, 1)
                for c in range(NC):
                    v.tensor_scalar(out=tmpc[:], in0=x_tm[:, c * D:(c + 1) * D],
                                    scalar1=st1[:, c:c + 1],
                                    scalar2=st1[:, 16 + c:17 + c],
                                    op0=OP.subtract, op1=OP.mult)
                    inst = v.tensor_tensor(out=r_tm[:, c * D:(c + 1) * D],
                                           in0=x_tm[:, c * D:(c + 1) * D],
                                           in1=tmpc[:], op=OP.add)
                    if c == NC - 1:
                        inst.then_inc(sm_xfv, 1)

            def _v_mid(v, tt, pv_done):
                tb = tt % 2
                if tt >= 1:
                    v.wait_ge(sm_op, 4 * tt)
                for n in range(NJ):
                    nb = n % 2
                    k = tt * NJ + n
                    v.wait_ge(sm_gX, k + 1)
                    v.wait_ge(sm_eSG, k + 1)
                    v.tensor_tensor(out=u_sb[:], in0=ppj[2][:], in1=sg_sb[:],
                                    op=OP.mult).then_inc(sm_u, 1)
                    v.wait_ge(sm_eA, k + 1)
                    if k >= 2:
                        v.wait_ge(sm_cy, k - 1)
                    v.tensor_tensor_scan(
                        out=h_sb[nb][:], data0=A_sb[nb][:], data1=u_sb[:],
                        initial=(0.0 if tt == 0 else carry[:, n:n + 1]),
                        op0=OP.mult, op1=OP.add).then_inc(sm_sc, 1)
                    v.wait_ge(sm_eG, k + 1)
                    v.tensor_tensor(out=y_sb[:, n * TT:(n + 1) * TT],
                                    in0=g_sb[nb][:], in1=h_sb[nb][:],
                                    op=OP.mult).then_inc(sm_y, 1)
                    q1n = min(q1(tt), P1N * (tt * NJ + n + 1))
                    q2n = min(q2q(tt), P2N * ((tt - CT - 1) * NJ + n + 1)) \
                        if tt > CT else 0
                    while pv_done[0] < q1n:
                        _prep_v1(v, pv_done[0]); pv_done[0] += 1
                    while pv_done[1] < q2n:
                        _prep_v2(v, pv_done[1]); pv_done[1] += 1

            def _v_evac1(v, tt, c):
                tb = tt % 2
                k = tt * NC + c
                v.wait_ge(sm_op, k + 1)
                if k >= 4:
                    v.wait_ge(sm_x2t, 16 * (k - 3))
                blo, bhi = pop[0], pop[1]
                v.tensor_tensor(out=x2c[c][:, 0:512], in0=blo[:],
                                in1=r_tm[:, c * D:c * D + 512], op=OP.add)
                v.tensor_tensor(out=x2c[c][:, 512:1024], in0=bhi[:],
                                in1=r_tm[:, c * D + 512:c * D + 1024],
                                op=OP.add).then_inc(sm_x2e, 1)

            def _v_tail(v, tt):
                for c in range(NC):
                    _bn_emit(v, st2, c, x2c[c][:])
                    v.tensor_reduce(out=st2[:, 8 + c:9 + c], in_=x2c[c][:],
                                    axis=AX, op=OP.max)
                    v.tensor_reduce(out=st2[:, 12 + c:13 + c], in_=x2c[c][:],
                                    axis=AX, op=OP.min)
                v.memset(st2[:, 88:92], 0.0).then_inc(sm_rv2, 1)
                v.wait_ge(sm_rs2, 2 * tt + 1)
                _bn_aggr_all(v, st2)
                for c in range(NC):
                    inst = v.tensor_scalar(out=st2[:, 16 + c:17 + c],
                                           in0=st2[:, 4 + c:5 + c], scalar1=EPS,
                                           scalar2=None, op0=OP.add)
                    if c == NC - 1:
                        inst.then_inc(sm_rv2, 1)
                v.wait_ge(sm_rs2, 2 * tt + 2)
                for c in range(NC):
                    v.reciprocal(out=st2[:, 20 + c:21 + c],
                                 in_=st2[:, 20 + c:21 + c])
                for c in range(NC):
                    v.tensor_tensor(out=st2[:, 24 + c:25 + c],
                                    in0=st2[:, 8 + c:9 + c],
                                    in1=st2[:, c:c + 1], op=OP.subtract)
                for c in range(NC):
                    v.tensor_tensor(out=st2[:, 28 + c:29 + c],
                                    in0=st2[:, c:c + 1],
                                    in1=st2[:, 12 + c:13 + c], op=OP.subtract)
                for c in range(NC):
                    v.tensor_tensor(out=st2[:, 24 + c:25 + c],
                                    in0=st2[:, 24 + c:25 + c],
                                    in1=st2[:, 28 + c:29 + c], op=OP.max)
                for c in range(NC):
                    v.tensor_tensor(out=st2[:, 28 + c:29 + c],
                                    in0=st2[:, 24 + c:25 + c],
                                    in1=st2[:, 20 + c:21 + c], op=OP.mult)
                for c in range(NC):
                    v.tensor_scalar(out=st2[:, 28 + c:29 + c],
                                    in0=st2[:, 28 + c:29 + c], scalar1=1e-5,
                                    scalar2=None, op0=OP.max)
                for c in range(NC):
                    v.reciprocal(out=st2[:, 32 + c:33 + c],
                                 in_=st2[:, 28 + c:29 + c])
                for c in range(NC):
                    k = tt * NC + c
                    v.tensor_scalar(out=s_all[:, k:k + 1],
                                    in0=st2[:, 32 + c:33 + c], scalar1=127.0,
                                    scalar2=None, op0=OP.mult)
                for c in range(NC):
                    k = tt * NC + c
                    v.tensor_tensor(out=st2[:, 28 + c:29 + c],
                                    in0=st2[:, 20 + c:21 + c],
                                    in1=s_all[:, k:k + 1], op=OP.mult)
                for c in range(NC):
                    k = tt * NC + c
                    if k >= 2:
                        v.wait_ge(sm_xqt, 16 * (k - 1))
                    v.tensor_scalar(out=tmpc[:], in0=x2c[c][:],
                                    scalar1=st2[:, c:c + 1],
                                    scalar2=st2[:, 28 + c:29 + c],
                                    op0=OP.subtract, op1=OP.mult)
                    v.tensor_scalar(out=xqc[c % 2][:], in0=tmpc[:],
                                    scalar1=MAGIC, scalar2=-MAGIC,
                                    op0=OP.add, op1=OP.add).then_inc(sm_xqv, 1)

            @blk.vector
            def _(v):
                v.wait_ge(ld0, 16 * NLD0)
                pv_done = [0, 0]
                for tt in range(NTT):
                    _v_head(v, tt, tt - 1 if tt >= 1 else None)
                    if tt >= 1:
                        _v_tail(v, tt - 1)
                    if tt == CT + 1:
                        _prep_vcmb(v)
                    _v_mid(v, tt, pv_done)
                for c in range(NC):
                    _v_evac1(v, NTT - 1, c)
                _v_tail(v, NTT - 1)
                while pv_done[1] < NWCH:
                    _prep_v2(v, pv_done[1]); pv_done[1] += 1
                for _ in range(4):
                    v.tensor_copy(out=wscr[:, 6:7], in_=wscr[:, 0:1])
                _prep_scales(v)

            def _s_ln2(s, tt):
                s.wait_ge(sm_rv2, 2 * tt + 1)
                s.nop().then_inc(sm_rs2, 1)
                s.wait_ge(sm_rv2, 2 * tt + 2)
                s.activation(out=st2[:, 20:24], in_=st2[:, 16:20],
                             func=AF.Sqrt).then_inc(sm_rs2, 1)

            @blk.scalar
            def _(s):
                s.wait_ge(ld0, 16 * NLD0)
                s2_done = [0]
                for tt in range(NTT):
                    tb = tt % 2
                    s.wait_ge(sm_rv1, 2 * tt + 1)
                    s.nop().then_inc(sm_rs1, 1)
                    s.wait_ge(sm_rv1, 2 * tt + 2)
                    s.activation(out=st1[:, 16:24], in_=st1[:, 8:16],
                                 func=AF.Sqrt).then_inc(sm_rs1, 1)
                    if tt >= 2:
                        s.wait_ge(sm_gG, (tt - 2) * NJ + NJ)
                    for c in range(NC):
                        for half in range(2):
                            bidx = tt * 8 + c * 2 + half
                            s.wait_ge(sm_xtr, bidx + 1)
                            pb = ptr[bidx % 2]
                            s.activation(
                                out=xnF[tb].rearrange("p (j t) -> p j t", j=NJ)[
                                    :, half * 4:half * 4 + 4, c * 128:(c + 1) * 128],
                                in_=pb.rearrange("p (j t) -> p j t", j=4)[:],
                                func=AF.Copy).then_inc(sm_xne, 1)
                    if tt >= 1:
                        _s_ln2(s, tt - 1)
                    if tt == CT + 1:
                        s.wait_ge(sm_pmm, 1)
                        s.activation(out=prow[0:1, 0:192], in_=ppj[0][0:1, 0:192],
                                     func=AF.Copy)
                        for _ in range(3):
                            s.activation(out=sspc[:, 0:1], in_=x_tm[:, 0:1],
                                         func=AF.Copy)
                        s.activation(out=sspc[:, 1:2], in_=x_tm[:, 0:1],
                                     func=AF.Copy).then_inc(sm_pmm, 1)
                    for n in range(NJ):
                        nb = n % 2
                        k = tt * NJ + n
                        s.wait_ge(sm_gA, k + 1)
                        if k >= 2:
                            s.wait_ge(sm_sc, k - 1)
                        s.activation(out=A_sb[nb][:], in_=ppj[0][:],
                                     func=AF.Sigmoid, bias=bA_c[:, n:n + 1],
                                     scale=1.0).then_inc(sm_eA, 1)
                        s.wait_ge(sm_gSG, k + 1)
                        if k >= 1:
                            s.wait_ge(sm_u, k)
                        s.activation(out=sg_sb[:], in_=ppj[1][:],
                                     func=AF.Sigmoid).then_inc(sm_eSG, 1)
                        s.wait_ge(sm_gG, k + 1)
                        if k >= 2:
                            s.wait_ge(sm_y, k - 1)
                        s.activation(out=g_sb[nb][:], in_=ppj[3][:],
                                     func=AF.Sigmoid).then_inc(sm_eG, 1)
                        s.wait_ge(sm_sc, k + 1)
                        s.activation(out=carry[:, n:n + 1],
                                     in_=h_sb[nb][:, TT - 1:TT],
                                     func=AF.Copy).then_inc(sm_cy, 1)
                        q2n = min(q2q(tt), P2N * ((tt - CT - 1) * NJ + n + 1)) \
                            if tt > CT else 0
                        while s2_done[0] < max(0, q2n - 2):
                            _prep_s2(s, s2_done[0]); s2_done[0] += 1
                _s_ln2(s, NTT - 1)
                while s2_done[0] < NWCH:
                    _prep_s2(s, s2_done[0]); s2_done[0] += 1

        p3 = contextlib.ExitStack()
        sb3, ps3 = mk(p3)
        with p3:
            xqF = [sb3(f"xqF{i}", [128, NJ * TT], BF16) for i in range(2)]
            x2t = sb3("x2t", [128, NC * D], F32)
            g2 = sb3("g2", [128, INNER], BF16)
            t1 = sb3("t1", [128, INNER], F32)
            q2 = [sb3(f"q2_{i}", [128, INNER], BF16) for i in range(2)]
            q2T = [sb3(f"q2T{i}", [128, INNER], BF16) for i in range(2)]
            x3st = [sb3(f"x3st{i}", [128, D], F32) for i in range(2)]
            wg8 = sb3("wg8", [128, NJ * INNER], FP8)
            wu8 = sb3("wu8", [128, NJ * INNER], FP8)
            wd8 = sb3("wd8", [128, NI * D], FP8)
            fcol = sb3("fcol", [128, 32], F32)
            pgu = [ps3(f"pgu{i}", [128, 512]) for i in range(4)]
            pdn = [ps3(f"pdn{i}", [128, 512]) for i in range(2)]
            ptq = [ps3(f"ptq{i}", [128, 512], BF16) for i in range(2)]

            def dpair(k):
                return (pdn[0], pdn[1])

            @blk.gpsimd
            def _(g):
                g.wait_ge(sm_pst, 16 * NWCH)
                g.dma_start(out=wg8[:].rearrange("p (k c) -> p k c", k=NJ),
                            in_=wq_h[0].rearrange("(k p) c -> p k c", p=128)
                            ).then_inc(sm_wq, 16)
                g.dma_start(out=wu8[:].rearrange("p (k c) -> p k c", k=NJ),
                            in_=wq_h[1].rearrange("(k p) c -> p k c", p=128)
                            ).then_inc(sm_wq, 16)
                g.dma_start(out=wd8[:].rearrange("p (k c) -> p k c", k=NI),
                            in_=wq_h[2].rearrange("(k p) c -> p k c", p=128)
                            ).then_inc(sm_wq, 16)

            def _xql(s, tt):
                s.wait_ge(sm_xqt, 16 * (tt * NC + NC))
                if tt >= 2:
                    s.wait_ge(sm_gu, (tt - 2) * NC * 8 + NC * 8)
                for j in range(NJ):
                    s.dma_start(
                        out=xqF[tt % 2][:, j * TT:(j + 1) * TT],
                        in_=xqs_h[tt * TT:(tt + 1) * TT, j * 128:(j + 1) * 128],
                        transpose=True).then_inc(sm_qtl, 16)

            @blk.sync
            def _(s):
                s.wait_ge(sm_pst, 16 * NWCH)
                _xql(s, 0)
                for tt in range(NTT):
                    if tt + 1 < NTT:
                        _xql(s, tt + 1)
                    if tt >= 1:
                        for n in range(NC):
                            k = (tt - 1) * NC + n
                            s.wait_ge(sm_dne, k + 1)
                            s.dma_start(out=out_h[k * 128:(k + 1) * 128, :],
                                        in_=x3st[k % 2][:]).then_inc(sm_ost, 16)
                    s.wait_ge(sm_x2t, 16 * (tt * NC + NC))
                    if tt >= 1:
                        s.wait_ge(sm_dne, tt * NC)
                    s.dma_start(out=x2t[:].rearrange("p (c d) -> p c d", c=NC),
                                in_=x2s_h[tt * TT:(tt + 1) * TT, :].rearrange(
                                    "(c p) d -> p c d", p=128)
                                ).then_inc(sm_x2l, 16)
                for n in range(NC):
                    k = (NTT - 1) * NC + n
                    s.wait_ge(sm_dne, k + 1)
                    s.dma_start(out=out_h[k * 128:(k + 1) * 128, :],
                                in_=x3st[k % 2][:]).then_inc(sm_ost, 16)
                s.wait_ge(sm_ost, 16 * NCH)

            def _transp(t, kq):
                t.wait_ge(sm_q2, kq + 1)
                for b in range(8):
                    if kq * 8 + b >= 2:
                        t.wait_ge(sm_q2e, kq * 8 + b - 1)
                    pb = ptq[b % 2]
                    for ii in range(4):
                        i = b * 4 + ii
                        inst = t.transpose(pb[:, ii * 128:(ii + 1) * 128],
                                           q2[kq % 2][:, i * 128:(i + 1) * 128],
                                           ident[:])
                        if ii == 3:
                            inst.then_inc(sm_q2t, 1)

            def _down(t, kd):
                if kd == 0:
                    t.wait_ge(sm_wq, 48)
                t.wait_ge(sm_q2e, kd * 8 + 8)
                if kd >= 1:
                    t.wait_ge(sm_dne, kd)
                pair = dpair(kd)
                for i in range(NI):
                    t.matmul(pair[0][:], q2T[kd % 2][:, i * 128:(i + 1) * 128],
                             wd8[:, i * D:i * D + 512],
                             start=(i == 0), stop=(i == NI - 1))
                    inst = t.matmul(pair[1][:],
                                    q2T[kd % 2][:, i * 128:(i + 1) * 128],
                                    wd8[:, i * D + 512:i * D + 1024],
                                    start=(i == 0), stop=(i == NI - 1))
                    if i == NI - 1:
                        inst.then_inc(sm_dn, 1)

            @blk.tensor
            def _(t):
                t.wait_ge(sm_wq, 32)
                t.wait_ge(sm_pv, 2 * NWCH + 2)
                for tt in range(NTT):
                    t.wait_ge(sm_qtl, 16 * (tt * NJ + NJ))
                    for n in range(NC):
                        k = tt * NC + n
                        for p in range(8):
                            ph = k * 8 + p
                            pg, pu = pgu[2 * (p % 2)], pgu[2 * (p % 2) + 1]
                            if ph >= 2:
                                t.wait_ge(sm_sil, ph - 1)
                                t.wait_ge(sm_t1, ph - 1)
                            for d in range(NJ):
                                col = d * INNER + p * 512
                                t.matmul(pg[:],
                                         xqF[tt % 2][:, d * TT + n * 128:
                                                     d * TT + (n + 1) * 128],
                                         wg8[:, col:col + 512],
                                         start=(d == 0), stop=(d == NJ - 1))
                                inst = t.matmul(
                                    pu[:],
                                    xqF[tt % 2][:, d * TT + n * 128:
                                                d * TT + (n + 1) * 128],
                                    wu8[:, col:col + 512],
                                    start=(d == 0), stop=(d == NJ - 1))
                                if d == NJ - 1:
                                    inst.then_inc(sm_gu, 1)
                        if k >= 2:
                            _down(t, k - 2)
                        if k >= 1:
                            _transp(t, k - 1)
                _transp(t, NCH - 1)
                _down(t, NCH - 2)
                _down(t, NCH - 1)

            def _devac(v, kd):
                v.wait_ge(sm_dn, kd + 1)
                v.wait_ge(sm_x2l, 16 * (kd // NC + 1))
                if kd >= 2:
                    v.wait_ge(sm_ost, 16 * (kd - 1))
                pair = dpair(kd)
                c = kd % NC
                fsl = fcol[:, 20 + (kd % 4):21 + (kd % 4)]
                v.scalar_tensor_tensor(out=x3st[kd % 2][:, 0:512], in0=pair[0][:],
                                       scalar=fsl, in1=x2t[:, c * D:c * D + 512],
                                       op0=OP.mult, op1=OP.add)
                v.scalar_tensor_tensor(out=x3st[kd % 2][:, 512:1024],
                                       in0=pair[1][:], scalar=fsl,
                                       in1=x2t[:, c * D + 512:c * D + 1024],
                                       op0=OP.mult, op1=OP.add).then_inc(sm_dne, 1)

            @blk.vector
            def _(v):
                v.memset(fcol[:, 24:32], 1.0)
                for tt in range(NTT):
                    for n in range(NC):
                        k = tt * NC + n
                        for p in range(8):
                            ph = k * 8 + p
                            pu = pgu[2 * (p % 2) + 1]
                            v.wait_ge(sm_gu, ph + 1)
                            v.wait_ge(sm_sil, ph + 1)
                            col = p * 512
                            v.tensor_tensor(out=t1[:, col:col + 512],
                                            in0=pu[:], in1=g2[:, col:col + 512],
                                            op=OP.mult).then_inc(sm_t1, 1)
                        v.tensor_reduce(out=fcol[:, 0:1], in_=t1[:], axis=AX,
                                        op=OP.max, apply_absolute_value=True)
                        if k >= 2:
                            _devac(v, k - 2)
                        v.tensor_copy(out=fcol[:, 28:32], in_=fcol[:, 24:28])
                        v.tensor_copy(out=fcol[:, 24:28], in_=fcol[:, 28:32])
                        v.tensor_tensor(out=fcol[:, 4:5], in0=fcol[:, 0:1],
                                        in1=su_all[:, k:k + 1], op=OP.mult)
                        v.tensor_copy(out=fcol[:, 28:32], in_=fcol[:, 24:28])
                        v.tensor_copy(out=fcol[:, 24:28], in_=fcol[:, 28:32])
                        v.tensor_scalar(out=fcol[:, 8:9], in0=fcol[:, 4:5],
                                        scalar1=1e-5, scalar2=None, op0=OP.max)
                        v.tensor_copy(out=fcol[:, 28:32], in_=fcol[:, 24:28])
                        v.tensor_copy(out=fcol[:, 24:28], in_=fcol[:, 28:32])
                        v.tensor_scalar(out=fcol[:, 20 + (k % 4):21 + (k % 4)],
                                        in0=fcol[:, 8:9],
                                        scalar1=swcols[:, 3:4], scalar2=None,
                                        op0=OP.mult)
                        v.reciprocal(out=fcol[:, 12:13], in_=fcol[:, 8:9])
                        v.tensor_copy(out=fcol[:, 28:32], in_=fcol[:, 24:28])
                        v.tensor_copy(out=fcol[:, 24:28], in_=fcol[:, 28:32])
                        v.tensor_scalar(out=fcol[:, 16:17], in0=fcol[:, 12:13],
                                        scalar1=127.0, scalar2=None, op0=OP.mult)
                        v.tensor_copy(out=fcol[:, 28:32], in_=fcol[:, 24:28])
                        v.tensor_copy(out=fcol[:, 24:28], in_=fcol[:, 28:32])
                        v.tensor_tensor(out=fcol[:, 16:17], in0=fcol[:, 16:17],
                                        in1=su_all[:, k:k + 1], op=OP.mult)
                        v.tensor_copy(out=fcol[:, 28:32], in_=fcol[:, 24:28])
                        v.tensor_copy(out=fcol[:, 24:28], in_=fcol[:, 28:32])
                        if k >= 2:
                            v.wait_ge(sm_q2t, (k - 2) * 8 + 8)
                        _t1b, _fb = bass.broadcast_tensor_aps(t1[:], fcol[:, 16:17])
                        v.scalar_tensor_tensor(out=t1[:], in0=t1[:], scalar=1.0,
                                               in1=_fb, op0=OP.mult, op1=OP.mult)
                        v.tensor_scalar(out=q2[k % 2][:], in0=t1[:],
                                        scalar1=MAGIC, scalar2=-MAGIC,
                                        op0=OP.add, op1=OP.add
                                        ).then_inc(sm_q2, 1)
                _devac(v, NCH - 2)
                _devac(v, NCH - 1)

            def _sevac(s, kq):
                if kq >= 2:
                    s.wait_ge(sm_dn, kq - 1)
                for b in range(8):
                    s.wait_ge(sm_q2t, kq * 8 + b + 1)
                    s.activation(
                        out=q2T[kq % 2].rearrange("p (i t) -> p i t", i=NI)[
                            :, b * 4:b * 4 + 4, 0:128],
                        in_=ptq[b % 2].rearrange("p (i t) -> p i t", i=4)[:],
                        func=AF.Copy).then_inc(sm_q2e, 1)

            @blk.scalar
            def _(s):
                for tt in range(NTT):
                    for n in range(NC):
                        k = tt * NC + n
                        for p in range(8):
                            ph = k * 8 + p
                            pg = pgu[2 * (p % 2)]
                            s.wait_ge(sm_gu, ph + 1)
                            if ph >= 1:
                                s.wait_ge(sm_t1, ph)
                            col = p * 512
                            s.activation(out=g2[:, col:col + 512], in_=pg[:],
                                         func=AF.Silu,
                                         scale=sg_all[:, k:k + 1]
                                         ).then_inc(sm_sil, 1)
                        if k >= 1:
                            _sevac(s, k - 1)
                _sevac(s, NCH - 1)

    return nc


def kernel(L0_attn_x, L0_attn_rms_weight, L0_attn_w_A, L0_attn_b_A, L0_attn_w_x,
           L0_attn_w_sidegate, L0_attn_w_output, L0_attn_w_o, L0_ffn_w_gate,
           L0_ffn_w_up, L0_ffn_w_down):
    x = np.ascontiguousarray(np.asarray(L0_attn_x, dtype=np.float32))
    B, L, _ = x.shape
    nc = build(L=L)
    base = {
        "wA": np.ascontiguousarray(np.asarray(L0_attn_w_A, np.float32)),
        "wSG": np.ascontiguousarray(np.asarray(L0_attn_w_output, np.float32)),
        "wX": np.ascontiguousarray(np.asarray(L0_attn_w_x, np.float32)),
        "wG": np.ascontiguousarray(np.asarray(L0_attn_w_sidegate, np.float32)),
        "wO": np.ascontiguousarray(np.asarray(L0_attn_w_o, np.float32)),
        "rmsT": np.ascontiguousarray(np.broadcast_to(
            np.asarray(L0_attn_rms_weight, np.float32), (128, D))),
        "bAc": np.ascontiguousarray(
            np.asarray(L0_attn_b_A, np.float32).reshape(NJ, 128).T),
        "wgT": np.ascontiguousarray(np.asarray(L0_ffn_w_gate, np.float32).T),
        "wuT": np.ascontiguousarray(np.asarray(L0_ffn_w_up, np.float32).T),
        "wdT": np.ascontiguousarray(np.asarray(L0_ffn_w_down, np.float32).T),
        "idb": np.eye(128, dtype=ml_dtypes.bfloat16),
        "ones1": np.ones((1, 128), np.float32),
        "onesc": np.ones((128, 1), np.float32),
    }
    in_maps = [dict(base, x=np.ascontiguousarray(x[c])) for c in range(B)]
    res = run_bass_kernel_spmd(nc, in_maps, core_ids=list(range(B)))
    return np.stack([np.asarray(res.results[c]["out"]) for c in range(B)])


# version 18 (final)
# speedup vs baseline: 1.0162x; 1.0162x over previous
import contextlib
import numpy as np
import ml_dtypes
import concourse.bass as bass
import concourse.mybir as mybir
from concourse.bass_utils import run_bass_kernel_spmd
from concourse.alu_op_type import AluOpType

F32 = mybir.dt.float32
BF16 = mybir.dt.bfloat16
FP8 = mybir.dt.float8e4
AF = mybir.ActivationFunctionType
OP = AluOpType
AX = mybir.AxisListType.X

D = 1024
INNER = 4096
NJ = D // 128
NI = INNER // 128
MAGIC = 12582912.0
EPS = 1e-5


def build(L=8192, TT=512, dbg=False):
    NTT = L // TT
    NC = TT // 128
    NCH = NTT * NC
    PRV = 4
    PCH = 64
    CW = 512
    NWCH = 3 * PCH
    nc = bass.Bass("TRN2", target_bir_lowering=False,
                   detect_race_conditions=False)

    x_h = nc.dram_tensor("x", [L, D], F32, kind="ExternalInput").ap()
    wA_h = nc.dram_tensor("wA", [D, D], F32, kind="ExternalInput").ap()
    wSG_h = nc.dram_tensor("wSG", [D, D], F32, kind="ExternalInput").ap()
    wX_h = nc.dram_tensor("wX", [D, D], F32, kind="ExternalInput").ap()
    wG_h = nc.dram_tensor("wG", [D, D], F32, kind="ExternalInput").ap()
    wO_h = nc.dram_tensor("wO", [D, D], F32, kind="ExternalInput").ap()
    rmsT_h = nc.dram_tensor("rmsT", [128, D], F32, kind="ExternalInput").ap()
    bA_h = nc.dram_tensor("bAc", [128, NJ], F32, kind="ExternalInput").ap()
    wgT_h = nc.dram_tensor("wgT", [D, INNER], F32, kind="ExternalInput").ap()
    wuT_h = nc.dram_tensor("wuT", [D, INNER], F32, kind="ExternalInput").ap()
    wdT_h = nc.dram_tensor("wdT", [INNER, D], F32, kind="ExternalInput").ap()
    idb_h = nc.dram_tensor("idb", [128, 128], BF16, kind="ExternalInput").ap()
    ones1_h = nc.dram_tensor("ones1", [1, 128], F32, kind="ExternalInput").ap()
    onesc_h = nc.dram_tensor("onesc", [128, 1], F32, kind="ExternalInput").ap()
    out_h = nc.dram_tensor("out", [L, D], F32, kind="ExternalOutput").ap()
    x2s_h = nc.dram_tensor("x2s", [L, D], F32,
                           kind="ExternalOutput" if dbg else "Internal").ap()
    xqs_h = nc.dram_tensor("xqs", [L, D], BF16,
                           kind="ExternalOutput" if dbg else "Internal").ap()
    _wqk = "ExternalOutput" if dbg else "Internal"
    wq_h = [nc.dram_tensor("wq_g", [D, INNER], FP8, kind=_wqk).ap(),
            nc.dram_tensor("wq_u", [D, INNER], FP8, kind=_wqk).ap(),
            nc.dram_tensor("wq_d", [INNER, D], FP8, kind=_wqk).ap()]

    cx = contextlib.ExitStack()

    def mk(stack):
        def sbf(name, shape, dt):
            return stack.enter_context(nc.sbuf_tensor(name, shape, dt))
        def psf(name, shape, dt=F32):
            return stack.enter_context(nc.psum_tensor(name, shape, dt))
        return sbf, psf

    sb, _ps = mk(cx)

    def sem(name):
        return cx.enter_context(nc.semaphore(name))

    with cx:
        ident = sb("ident", [128, 128], BF16)
        ones1 = sb("ones1s", [1, 128], F32)
        onesc = sb("onescs", [128, 1], F32)
        s_all = sb("s_all", [128, NCH], F32)
        sg_all = sb("sg_all", [128, NCH], F32)
        su_all = sb("su_all", [128, NCH], F32)
        swcols = sb("swcols", [128, 8], F32)
        ld0 = sem("ld0")
        NLD0 = 10

        sm_xld = sem("s_xld"); sm_xfv = sem("s_xfv")
        sm_sq1 = sem("s_sq1"); sm_rqv = sem("s_rqv"); sm_rqs = sem("s_rqs")
        sm_rv1 = sem("s_rv1"); sm_rs1 = sem("s_rs1")
        sm_rv2 = sem("s_rv2"); sm_rs2 = sem("s_rs2")
        sm_xn = sem("s_xn"); sm_xtr = sem("s_xtr"); sm_xne = sem("s_xne")
        sm_gA = sem("s_gA"); sm_gSG = sem("s_gSG")
        sm_gX = sem("s_gX"); sm_gG = sem("s_gG")
        sm_eA = sem("s_eA"); sm_eSG = sem("s_eSG"); sm_eG = sem("s_eG")
        sm_u = sem("s_u"); sm_sc = sem("s_sc"); sm_cy = sem("s_cy")
        sm_y = sem("s_y"); sm_op = sem("s_op"); sm_x2e = sem("s_x2e")
        sm_x2t = sem("s_x2t"); sm_xqv = sem("s_xqv"); sm_xqt = sem("s_xqt")
        sm_sq2 = sem("s_sq2")
        sm_pld = sem("s_pld"); sm_pv = sem("s_pv"); sm_ps2 = sem("s_ps2")
        sm_pst = sem("s_pst"); sm_pmm = sem("s_pmm")
        sm_opl = sem("s_opl"); sm_x2lo = sem("s_x2lo")
        sm_wq = sem("s_wq"); sm_qtl = sem("s_qtl"); sm_x2l = sem("s_x2l")
        sm_gu = sem("s_gu"); sm_sil = sem("s_sil"); sm_t1 = sem("s_t1")
        sm_q2 = sem("s_q2"); sm_q2t = sem("s_q2t"); sm_q2e = sem("s_q2e")
        sm_dn = sem("s_dn"); sm_dne = sem("s_dne"); sm_ost = sem("s_ost")
        sm_vf = sem("s_vf"); sm_sf = sem("s_sf")

        blk = cx.enter_context(nc.Block())

        def pv2(j):
            return NWCH + 1 + j + 1

        p1 = contextlib.ExitStack()
        sb1, ps1 = mk(p1)
        with p1:
            x_tm = sb1("x_tm", [128, NC * D], F32)
            pst = [sb1(f"pst{i}", [128, 512], F32) for i in range(4)]
            pz = [sb1(f"pz{i}", [128, 512], F32) for i in range(3)]
            pq8 = [sb1(f"pq8{i}", [128, 512], FP8) for i in range(2)]
            mcol = sb1("mcol", [128, 192], F32)
            sspc = sb1("sspc", [128, 8], F32)
            prow = sb1("prow", [1, 192], F32)
            wscr = sb1("wscr", [128, 8], F32)
            xn_tm = sb1("xn_tm", [128, NC * D], BF16)
            r_tm = sb1("r_tm", [128, NC * D], F32)
            xnF = [sb1(f"xnF{i}", [128, NJ * TT], BF16) for i in range(2)]
            wP = [sb1(f"wP{i}", [128, NJ * D], BF16) for i in range(4)]
            wo_s = sb1("wo_s", [128, NJ * D], BF16)
            rms_t = sb1("rms_t", [128, D], F32)
            bA_c = sb1("bA_c", [128, NJ], F32)
            A_sb = [sb1(f"A_sb{i}", [128, TT], F32) for i in range(2)]
            sg_sb = sb1("sg_sb", [128, TT], F32)
            g_sb = [sb1(f"g_sb{i}", [128, TT], F32) for i in range(2)]
            u_sb = sb1("u_sb", [128, TT], F32)
            h_sb = [sb1(f"h_sb{i}", [128, TT], F32) for i in range(2)]
            y_sb = sb1("y_sb", [128, NJ * TT], BF16)
            carry = sb1("carry", [128, NJ], F32)
            x2c = [sb1(f"x2c{i}", [128, D], F32) for i in range(4)]
            xqc = [sb1(f"xqc{i}", [128, D], BF16) for i in range(2)]
            st1 = sb1("st1", [128, 112], F32)
            st2 = sb1("st2", [128, 112], F32)
            tmpc = sb1("tmpc", [128, D], F32)
            ptr = [ps1(f"ptr{i}", [128, 512], BF16) for i in range(2)]
            ppj = [ps1(f"ppj{i}", [128, 512]) for i in range(4)]
            pop = [ps1(f"pop{i}", [128, 512]) for i in range(2)]

            WSRC = [wgT_h, wuT_h, wdT_h]

            def chunk(w, k, hbm):
                if w < 2:
                    r0, c0 = (k // 8) * 128, (k % 8) * CW
                else:
                    r0, c0 = (k // 2) * 128, (k % 2) * CW
                return hbm[r0:r0 + 128, c0:c0 + CW]

            def _prep_ld(s, j, rnd):
                w, k = j // PCH, j % PCH
                idx = rnd * NWCH + j
                if rnd == 0:
                    if idx >= 4:
                        s.wait_ge(sm_pv, idx - 3)
                else:
                    s.wait_ge(sm_pv, NWCH if j < 4 else pv2(j - 4))
                s.dma_start(out=pst[idx % 4][:],
                            in_=chunk(w, k, WSRC[w])).then_inc(sm_pld, 16)

            def _prep_v1(v, j):
                v.wait_ge(sm_pld, 16 * (j + 1))
                inst = v.tensor_reduce(out=mcol[:, j:j + 1], in_=pst[j % 4][:],
                                       axis=AX, op=OP.add,
                                       apply_absolute_value=True)
                if j < NWCH - 1:
                    inst.then_inc(sm_pv, 1)
                else:
                    for _ in range(4):
                        v.tensor_copy(out=wscr[:, 6:7], in_=mcol[:, 0:1])
                    v.tensor_copy(out=wscr[:, 7:8], in_=mcol[:, 0:1]
                                  ).then_inc(sm_pv, 1)

            def _prep_v2(v, j):
                idx = NWCH + j
                v.wait_ge(sm_pld, 16 * (idx + 1))
                if j >= 3:
                    v.wait_ge(sm_ps2, j - 2)
                w = j // PCH
                v.tensor_scalar(out=pz[j % 3][:], in0=pst[idx % 4][:],
                                scalar1=swcols[:, w:w + 1], scalar2=MAGIC,
                                op0=OP.mult, op1=OP.add)
                v.tensor_scalar(out=pz[j % 3][:], in0=pz[j % 3][:],
                                scalar1=MAGIC - 1.0, scalar2=MAGIC + 1.0,
                                op0=OP.max, op1=OP.min).then_inc(sm_pv, 1)

            def _prep_s2(s, j):
                w, k = j // PCH, j % PCH
                s.wait_ge(sm_pv, min(pv2(j) + 2, 2 * NWCH + 2))
                if j >= 2:
                    s.wait_ge(sm_pst, 16 * (j - 1))
                s.activation(out=pq8[j % 2][:], in_=pz[j % 3][:], func=AF.Copy,
                             bias=-MAGIC).then_inc(sm_ps2, 1)
                s.dma_start(out=chunk(w, k, wq_h[w]),
                            in_=pq8[j % 2][:]).then_inc(sm_pst, 16)

            def _prep_vcmb(v):
                v.wait_ge(sm_pmm, 3)
                for w in range(3):
                    v.tensor_reduce(out=wscr[:, w:w + 1],
                                    in_=ppj[1][:, w * 64:(w + 1) * 64],
                                    axis=AX, op=OP.add)
                v.tensor_copy(out=wscr[:, 6:7], in_=wscr[:, 0:1])
                v.tensor_copy(out=wscr[:, 7:8], in_=wscr[:, 0:1])
                v.tensor_scalar(out=wscr[:, 3:6], in0=wscr[:, 0:3],
                                scalar1=1.0 / (D * INNER), scalar2=1e-5,
                                op0=OP.mult, op1=OP.max)
                v.tensor_copy(out=wscr[:, 6:7], in_=wscr[:, 0:1])
                v.tensor_copy(out=wscr[:, 7:8], in_=wscr[:, 0:1])
                v.reciprocal(out=swcols[:, 0:3], in_=wscr[:, 3:6])
                v.tensor_scalar(out=swcols[:, 3:4], in0=wscr[:, 5:6],
                                scalar1=1.0 / 127.0, scalar2=None,
                                op0=OP.mult)
                for _ in range(4):
                    v.tensor_copy(out=wscr[:, 6:7], in_=mcol[:, 0:1])
                v.tensor_copy(out=wscr[:, 7:8], in_=mcol[:, 0:1]
                              ).then_inc(sm_pv, 1)

            def _prep_scales(v):
                v.tensor_scalar(out=sg_all[:], in0=s_all[:],
                                scalar1=swcols[:, 0:1], scalar2=None, op0=OP.mult)
                v.tensor_scalar(out=su_all[:], in0=s_all[:],
                                scalar1=swcols[:, 1:2], scalar2=None, op0=OP.mult)
                v.tensor_copy(out=wscr[:, 6:7], in_=wscr[:, 0:1])
                v.tensor_copy(out=wscr[:, 7:8], in_=wscr[:, 0:1])
                v.reciprocal(out=sg_all[:], in_=sg_all[:])
                v.reciprocal(out=su_all[:], in_=su_all[:]).then_inc(sm_pv, 1)

            CT = min(7, NTT - 2)
            P1N = -(-NWCH // ((CT + 1) * NJ))
            P2N = -(-NWCH // ((NTT - 1 - CT) * NJ))

            def q1(tt):
                if tt >= CT:
                    return NWCH
                return min(NWCH, P1N * NJ * (tt + 1))

            def q2q(tt):
                if tt <= CT:
                    return 0
                return min(NWCH, P2N * NJ * (tt - CT))



            @blk.gpsimd
            def _(g):
                def r3(t):
                    return t[:].rearrange("p (k n) -> p k n", k=NJ)
                for dst, src in [
                        (ident[:], idb_h), (ones1[:], ones1_h), (onesc[:], onesc_h),
                        (rms_t[:], rmsT_h), (bA_c[:], bA_h),
                        (r3(wP[0]), wA_h.rearrange("(k p) n -> p k n", p=128)),
                        (r3(wP[1]), wSG_h.rearrange("(k p) n -> p k n", p=128)),
                        (r3(wP[2]), wX_h.rearrange("(k p) n -> p k n", p=128)),
                        (r3(wP[3]), wG_h.rearrange("(k p) n -> p k n", p=128)),
                        (r3(wo_s), wO_h.rearrange("(k p) n -> p k n", p=128))]:
                    g.dma_start(out=dst, in_=src).then_inc(ld0, 16)
                for k in range(NTT * NC):
                    g.wait_ge(sm_xqv, k + 1)
                    g.dma_start(out=xqs_h[k * 128:(k + 1) * 128, :],
                                in_=xqc[k % 2][:]).then_inc(sm_xqt, 16)

            @blk.sync
            def _(s):
                xv = x_h.rearrange("(tt c p) d -> tt p c d", c=NC, p=128)
                ld_done = [0, 0]
                for tt in range(NTT):
                    if tt >= 1:
                        s.wait_ge(sm_xfv, tt)
                    s.dma_start(out=x_tm[:].rearrange("p (c d) -> p c d", c=NC), in_=xv[tt]).then_inc(sm_xld, 16)
                    while ld_done[0] < q1(tt):
                        _prep_ld(s, ld_done[0], 0); ld_done[0] += 1
                    while ld_done[1] < q2q(tt):
                        _prep_ld(s, ld_done[1], 1); ld_done[1] += 1
                    if tt >= 1:
                        for c in range(NC):
                            k = (tt - 1) * NC + c
                            s.wait_ge(sm_x2e, k + 1)
                            s.dma_start(out=x2s_h[k * 128:(k + 1) * 128, :],
                                        in_=x2c[c][:]).then_inc(sm_x2t, 16)
                for c in range(NC):
                    k = (NTT - 1) * NC + c
                    s.wait_ge(sm_x2e, k + 1)
                    s.dma_start(out=x2s_h[k * 128:(k + 1) * 128, :],
                                in_=x2c[c][:]).then_inc(sm_x2t, 16)

            @blk.tensor
            def _(t):
                t.wait_ge(ld0, 16 * NLD0)
                for tt in range(NTT):
                    tb = tt % 2
                    for c in range(NC):
                        t.wait_ge(sm_xn, tt * NC + c + 1)
                        for half in range(2):
                            bidx = tt * 8 + c * 2 + half
                            if bidx >= 2:
                                t.wait_ge(sm_xne, bidx - 1)
                            pb = ptr[bidx % 2]
                            for jj in range(4):
                                j = half * 4 + jj
                                inst = t.transpose(
                                    pb[:, jj * 128:(jj + 1) * 128],
                                    xn_tm[:, c * D + j * 128:c * D + (j + 1) * 128],
                                    ident[:])
                                if jj == 3:
                                    inst.then_inc(sm_xtr, 1)
                    t.wait_ge(sm_xne, tt * 8 + 8)
                    if tt == CT + 1:
                        t.wait_ge(sm_pmm, 2)
                        t.wait_ge(sm_eSG, (CT + 1) * NJ)
                        t.matmul(ppj[1][:, 0:192], ones1[:], prow[0:1, 0:192],
                                 start=True, stop=True).then_inc(sm_pmm, 1)
                        t.wait_ge(sm_pv, NWCH + 1)
                    for n in range(NJ):
                        k = tt * NJ + n
                        t.wait_ge(sm_eA, k)
                        for d in range(NJ):
                            inst = t.matmul(
                                ppj[0][:],
                                wP[0][:, d * D + n * 128:d * D + (n + 1) * 128],
                                xnF[tb][:, d * TT:(d + 1) * TT],
                                start=(d == 0), stop=(d == NJ - 1))
                            if d == NJ - 1:
                                inst.then_inc(sm_gA, 1)
                        t.wait_ge(sm_eSG, k)
                        for d in range(NJ):
                            inst = t.matmul(
                                ppj[1][:],
                                wP[1][:, d * D + n * 128:d * D + (n + 1) * 128],
                                xnF[tb][:, d * TT:(d + 1) * TT],
                                start=(d == 0), stop=(d == NJ - 1))
                            if d == NJ - 1:
                                inst.then_inc(sm_gSG, 1)
                        t.wait_ge(sm_u, k)
                        for d in range(NJ):
                            inst = t.matmul(
                                ppj[2][:],
                                wP[2][:, d * D + n * 128:d * D + (n + 1) * 128],
                                xnF[tb][:, d * TT:(d + 1) * TT],
                                start=(d == 0), stop=(d == NJ - 1))
                            if d == NJ - 1:
                                inst.then_inc(sm_gX, 1)
                        t.wait_ge(sm_eG, k)
                        for d in range(NJ):
                            inst = t.matmul(
                                ppj[3][:],
                                wP[3][:, d * D + n * 128:d * D + (n + 1) * 128],
                                xnF[tb][:, d * TT:(d + 1) * TT],
                                start=(d == 0), stop=(d == NJ - 1))
                            if d == NJ - 1:
                                inst.then_inc(sm_gG, 1)
                    t.wait_ge(sm_y, tt * NJ + NJ)
                    for c in range(NC):
                        k = tt * NC + c
                        if k >= 1:
                            t.wait_ge(sm_x2lo, k)
                        blo, bhi = pop[0], pop[1]
                        for n in range(NJ):
                            inst = t.matmul(
                                blo[:],
                                y_sb[:, n * TT + c * 128:n * TT + (c + 1) * 128],
                                wo_s[:, n * D:n * D + 512],
                                start=(n == 0), stop=(n == NJ - 1))
                            if n == NJ - 1:
                                inst.then_inc(sm_opl, 1)
                        if k >= 1:
                            t.wait_ge(sm_x2e, k)
                        for n in range(NJ):
                            inst = t.matmul(
                                bhi[:],
                                y_sb[:, n * TT + c * 128:n * TT + (c + 1) * 128],
                                wo_s[:, n * D + 512:n * D + 1024],
                                start=(n == 0), stop=(n == NJ - 1))
                            if n == NJ - 1:
                                inst.then_inc(sm_op, 1)
                    if tt == CT:
                        t.wait_ge(sm_pv, NWCH)
                        t.wait_ge(sm_eA, (CT + 1) * NJ)
                        t.matmul(ppj[0][0:1, 0:192], onesc[:], mcol[:],
                                 start=True, stop=True).then_inc(sm_pmm, 1)

            def _bn_emit(v, st, c, xin):
                b0 = 36 + 12 * c
                v.bn_stats(out=st[:, b0:b0 + 6], in_=xin[:, 0:512])
                v.bn_stats(out=st[:, b0 + 6:b0 + 12], in_=xin[:, 512:1024])

            def _bn_aggr_all(v, st):
                for c in range(NC):
                    b0 = 36 + 12 * c
                    v.tensor_reduce(out=st[:, c:c + 1],
                                    in_=st[:, b0 + 1:b0 + 12:3], axis=AX, op=OP.add)
                for c in range(NC):
                    b0 = 36 + 12 * c
                    v.tensor_reduce(out=st[:, 100 + c:101 + c],
                                    in_=st[:, b0 + 2:b0 + 12:3], axis=AX, op=OP.add)
                for c in range(NC):
                    b0 = 36 + 12 * c
                    v.tensor_tensor(out=st[:, 84 + 4 * c:88 + 4 * c],
                                    in0=st[:, b0 + 1:b0 + 12:3],
                                    in1=st[:, b0 + 1:b0 + 12:3], op=OP.mult)
                for c in range(NC):
                    v.tensor_reduce(out=st[:, 104 + c:105 + c],
                                    in_=st[:, 84 + 4 * c:88 + 4 * c],
                                    axis=AX, op=OP.add)
                for c in range(NC):
                    v.tensor_scalar(out=st[:, c:c + 1], in0=st[:, c:c + 1],
                                    scalar1=0.25, scalar2=None, op0=OP.mult)
                for c in range(NC):
                    v.scalar_tensor_tensor(out=st[:, 100 + c:101 + c],
                                           in0=st[:, 104 + c:105 + c],
                                           scalar=float(D // 4),
                                           in1=st[:, 100 + c:101 + c],
                                           op0=OP.mult, op1=OP.add)
                for c in range(NC):
                    v.tensor_tensor(out=st[:, 108 + c:109 + c], in0=st[:, c:c + 1],
                                    in1=st[:, c:c + 1], op=OP.mult)
                for c in range(NC):
                    v.scalar_tensor_tensor(out=st[:, 4 + c:5 + c],
                                           in0=st[:, 100 + c:101 + c],
                                           scalar=1.0 / D,
                                           in1=st[:, 108 + c:109 + c],
                                           op0=OP.mult, op1=OP.subtract)

            def _v_head(v, tt, ev):
                tb = tt % 2
                v.wait_ge(sm_xld, 16 * (tt + 1))
                if tt >= 1:
                    v.wait_ge(sm_xtr, (tt - 1) * 8 + 8)
                for c in range(NC):
                    _bn_emit(v, st1, c, x_tm[:, c * D:(c + 1) * D])
                    if ev is not None:
                        _v_evac1(v, ev, c)
                v.memset(st1[:, 88:92], 0.0).then_inc(sm_rv1, 1)
                v.wait_ge(sm_rs1, 2 * tt + 1)
                _bn_aggr_all(v, st1)
                for c in range(NC):
                    v.tensor_scalar(out=st1[:, 8 + c:9 + c],
                                    in0=st1[:, 4 + c:5 + c], scalar1=EPS,
                                    scalar2=None, op0=OP.add)
                for c in range(NC):
                    v.reciprocal(out=st1[:, 28 + c:29 + c], in_=st1[:, 8 + c:9 + c])
                for c in range(NC):
                    v.tensor_tensor(out=st1[:, 32 + c:33 + c],
                                    in0=st1[:, 4 + c:5 + c],
                                    in1=st1[:, 28 + c:29 + c], op=OP.mult)
                for c in range(NC):
                    inst = v.tensor_scalar(out=st1[:, 12 + c:13 + c],
                                           in0=st1[:, 32 + c:33 + c],
                                           scalar1=EPS, scalar2=None, op0=OP.add)
                    if c == NC - 1:
                        inst.then_inc(sm_rv1, 1)
                v.wait_ge(sm_rs1, 2 * tt + 2)
                v.tensor_tensor(out=st1[:, 24:28], in0=st1[:, 16:20],
                                in1=st1[:, 20:24], op=OP.mult)
                v.tensor_copy(out=st1[:, 92:96], in_=st1[:, 88:92])
                v.tensor_copy(out=st1[:, 88:92], in_=st1[:, 92:96])
                v.reciprocal(out=st1[:, 24:28], in_=st1[:, 24:28])
                v.reciprocal(out=st1[:, 16:20], in_=st1[:, 16:20])
                v.tensor_copy(out=st1[:, 92:96], in_=st1[:, 88:92])
                v.tensor_copy(out=st1[:, 88:92], in_=st1[:, 92:96])
                for c in range(NC):
                    v.tensor_scalar(out=tmpc[:], in0=x_tm[:, c * D:(c + 1) * D],
                                    scalar1=st1[:, c:c + 1],
                                    scalar2=st1[:, 24 + c:25 + c],
                                    op0=OP.subtract, op1=OP.mult)
                    v.tensor_tensor(out=xn_tm[:, c * D:(c + 1) * D],
                                    in0=tmpc[:], in1=rms_t[:],
                                    op=OP.mult).then_inc(sm_xn, 1)
                    v.tensor_scalar(out=tmpc[:], in0=x_tm[:, c * D:(c + 1) * D],
                                    scalar1=st1[:, c:c + 1],
                                    scalar2=st1[:, 16 + c:17 + c],
                                    op0=OP.subtract, op1=OP.mult)
                    inst = v.tensor_tensor(out=r_tm[:, c * D:(c + 1) * D],
                                           in0=x_tm[:, c * D:(c + 1) * D],
                                           in1=tmpc[:], op=OP.add)
                    if c == NC - 1:
                        inst.then_inc(sm_xfv, 1)

            def _v_mid(v, tt, pv_done):
                tb = tt % 2
                if tt >= 1:
                    v.wait_ge(sm_op, 4 * tt)
                for n in range(NJ):
                    nb = n % 2
                    k = tt * NJ + n
                    v.wait_ge(sm_gX, k + 1)
                    v.wait_ge(sm_eSG, k + 1)
                    v.tensor_tensor(out=u_sb[:], in0=ppj[2][:], in1=sg_sb[:],
                                    op=OP.mult).then_inc(sm_u, 1)
                    v.wait_ge(sm_eA, k + 1)
                    if k >= 2:
                        v.wait_ge(sm_cy, k - 1)
                    v.tensor_tensor_scan(
                        out=h_sb[nb][:], data0=A_sb[nb][:], data1=u_sb[:],
                        initial=(0.0 if tt == 0 else carry[:, n:n + 1]),
                        op0=OP.mult, op1=OP.add).then_inc(sm_sc, 1)
                    v.wait_ge(sm_eG, k + 1)
                    v.tensor_tensor(out=y_sb[:, n * TT:(n + 1) * TT],
                                    in0=g_sb[nb][:], in1=h_sb[nb][:],
                                    op=OP.mult).then_inc(sm_y, 1)
                    q1n = min(q1(tt), P1N * (tt * NJ + n + 1))
                    q2n = min(q2q(tt), P2N * ((tt - CT - 1) * NJ + n + 1)) \
                        if tt > CT else 0
                    while pv_done[0] < q1n:
                        _prep_v1(v, pv_done[0]); pv_done[0] += 1
                    while pv_done[1] < q2n:
                        _prep_v2(v, pv_done[1]); pv_done[1] += 1

            def _v_evac1(v, tt, c):
                tb = tt % 2
                k = tt * NC + c
                v.wait_ge(sm_opl, k + 1)
                if k >= 4:
                    v.wait_ge(sm_x2t, 16 * (k - 3))
                blo, bhi = pop[0], pop[1]
                v.tensor_tensor(out=x2c[c][:, 0:512], in0=blo[:],
                                in1=r_tm[:, c * D:c * D + 512],
                                op=OP.add).then_inc(sm_x2lo, 1)
                v.wait_ge(sm_op, k + 1)
                v.tensor_tensor(out=x2c[c][:, 512:1024], in0=bhi[:],
                                in1=r_tm[:, c * D + 512:c * D + 1024],
                                op=OP.add).then_inc(sm_x2e, 1)

            def _v_tail(v, tt):
                for c in range(NC):
                    _bn_emit(v, st2, c, x2c[c][:])
                    v.tensor_reduce(out=st2[:, 8 + c:9 + c], in_=x2c[c][:],
                                    axis=AX, op=OP.max)
                    v.tensor_reduce(out=st2[:, 12 + c:13 + c], in_=x2c[c][:],
                                    axis=AX, op=OP.min)
                v.memset(st2[:, 88:92], 0.0).then_inc(sm_rv2, 1)
                v.wait_ge(sm_rs2, 2 * tt + 1)
                _bn_aggr_all(v, st2)
                for c in range(NC):
                    inst = v.tensor_scalar(out=st2[:, 16 + c:17 + c],
                                           in0=st2[:, 4 + c:5 + c], scalar1=EPS,
                                           scalar2=None, op0=OP.add)
                    if c == NC - 1:
                        inst.then_inc(sm_rv2, 1)
                v.wait_ge(sm_rs2, 2 * tt + 2)
                for c in range(NC):
                    v.reciprocal(out=st2[:, 20 + c:21 + c],
                                 in_=st2[:, 20 + c:21 + c])
                for c in range(NC):
                    v.tensor_tensor(out=st2[:, 24 + c:25 + c],
                                    in0=st2[:, 8 + c:9 + c],
                                    in1=st2[:, c:c + 1], op=OP.subtract)
                for c in range(NC):
                    v.tensor_tensor(out=st2[:, 28 + c:29 + c],
                                    in0=st2[:, c:c + 1],
                                    in1=st2[:, 12 + c:13 + c], op=OP.subtract)
                for c in range(NC):
                    v.tensor_tensor(out=st2[:, 24 + c:25 + c],
                                    in0=st2[:, 24 + c:25 + c],
                                    in1=st2[:, 28 + c:29 + c], op=OP.max)
                for c in range(NC):
                    v.tensor_tensor(out=st2[:, 28 + c:29 + c],
                                    in0=st2[:, 24 + c:25 + c],
                                    in1=st2[:, 20 + c:21 + c], op=OP.mult)
                for c in range(NC):
                    v.tensor_scalar(out=st2[:, 28 + c:29 + c],
                                    in0=st2[:, 28 + c:29 + c], scalar1=1e-5,
                                    scalar2=None, op0=OP.max)
                for c in range(NC):
                    v.reciprocal(out=st2[:, 32 + c:33 + c],
                                 in_=st2[:, 28 + c:29 + c])
                for c in range(NC):
                    k = tt * NC + c
                    v.tensor_scalar(out=s_all[:, k:k + 1],
                                    in0=st2[:, 32 + c:33 + c], scalar1=127.0,
                                    scalar2=None, op0=OP.mult)
                for c in range(NC):
                    k = tt * NC + c
                    v.tensor_tensor(out=st2[:, 28 + c:29 + c],
                                    in0=st2[:, 20 + c:21 + c],
                                    in1=s_all[:, k:k + 1], op=OP.mult)
                for c in range(NC):
                    k = tt * NC + c
                    if k >= 2:
                        v.wait_ge(sm_xqt, 16 * (k - 1))
                    v.tensor_scalar(out=tmpc[:], in0=x2c[c][:],
                                    scalar1=st2[:, c:c + 1],
                                    scalar2=st2[:, 28 + c:29 + c],
                                    op0=OP.subtract, op1=OP.mult)
                    v.tensor_scalar(out=xqc[c % 2][:], in0=tmpc[:],
                                    scalar1=MAGIC, scalar2=-MAGIC,
                                    op0=OP.add, op1=OP.add).then_inc(sm_xqv, 1)

            @blk.vector
            def _(v):
                v.wait_ge(ld0, 16 * NLD0)
                pv_done = [0, 0]
                for tt in range(NTT):
                    _v_head(v, tt, tt - 1 if tt >= 1 else None)
                    if tt >= 1:
                        _v_tail(v, tt - 1)
                    if tt == CT + 1:
                        _prep_vcmb(v)
                    _v_mid(v, tt, pv_done)
                for c in range(NC):
                    _v_evac1(v, NTT - 1, c)
                _v_tail(v, NTT - 1)
                while pv_done[1] < NWCH:
                    _prep_v2(v, pv_done[1]); pv_done[1] += 1
                for _ in range(4):
                    v.tensor_copy(out=wscr[:, 6:7], in_=wscr[:, 0:1])
                _prep_scales(v)

            def _s_ln2(s, tt):
                s.wait_ge(sm_rv2, 2 * tt + 1)
                s.nop().then_inc(sm_rs2, 1)
                s.wait_ge(sm_rv2, 2 * tt + 2)
                s.activation(out=st2[:, 20:24], in_=st2[:, 16:20],
                             func=AF.Sqrt).then_inc(sm_rs2, 1)

            @blk.scalar
            def _(s):
                s.wait_ge(ld0, 16 * NLD0)
                s2_done = [0]
                for tt in range(NTT):
                    tb = tt % 2
                    s.wait_ge(sm_rv1, 2 * tt + 1)
                    s.nop().then_inc(sm_rs1, 1)
                    s.wait_ge(sm_rv1, 2 * tt + 2)
                    s.activation(out=st1[:, 16:24], in_=st1[:, 8:16],
                                 func=AF.Sqrt).then_inc(sm_rs1, 1)
                    if tt >= 2:
                        s.wait_ge(sm_gG, (tt - 2) * NJ + NJ)
                    for c in range(NC):
                        for half in range(2):
                            bidx = tt * 8 + c * 2 + half
                            s.wait_ge(sm_xtr, bidx + 1)
                            pb = ptr[bidx % 2]
                            s.activation(
                                out=xnF[tb].rearrange("p (j t) -> p j t", j=NJ)[
                                    :, half * 4:half * 4 + 4, c * 128:(c + 1) * 128],
                                in_=pb.rearrange("p (j t) -> p j t", j=4)[:],
                                func=AF.Copy).then_inc(sm_xne, 1)
                    if tt >= 1:
                        _s_ln2(s, tt - 1)
                    if tt == CT + 1:
                        s.wait_ge(sm_pmm, 1)
                        s.activation(out=prow[0:1, 0:192], in_=ppj[0][0:1, 0:192],
                                     func=AF.Copy)
                        for _ in range(3):
                            s.activation(out=sspc[:, 0:1], in_=x_tm[:, 0:1],
                                         func=AF.Copy)
                        s.activation(out=sspc[:, 1:2], in_=x_tm[:, 0:1],
                                     func=AF.Copy).then_inc(sm_pmm, 1)
                    for n in range(NJ):
                        nb = n % 2
                        k = tt * NJ + n
                        s.wait_ge(sm_gA, k + 1)
                        if k >= 2:
                            s.wait_ge(sm_sc, k - 1)
                        s.activation(out=A_sb[nb][:], in_=ppj[0][:],
                                     func=AF.Sigmoid, bias=bA_c[:, n:n + 1],
                                     scale=1.0).then_inc(sm_eA, 1)
                        s.wait_ge(sm_gSG, k + 1)
                        if k >= 1:
                            s.wait_ge(sm_u, k)
                        s.activation(out=sg_sb[:], in_=ppj[1][:],
                                     func=AF.Sigmoid).then_inc(sm_eSG, 1)
                        s.wait_ge(sm_gG, k + 1)
                        if k >= 2:
                            s.wait_ge(sm_y, k - 1)
                        s.activation(out=g_sb[nb][:], in_=ppj[3][:],
                                     func=AF.Sigmoid).then_inc(sm_eG, 1)
                        s.wait_ge(sm_sc, k + 1)
                        s.activation(out=carry[:, n:n + 1],
                                     in_=h_sb[nb][:, TT - 1:TT],
                                     func=AF.Copy).then_inc(sm_cy, 1)
                        q2n = min(q2q(tt), P2N * ((tt - CT - 1) * NJ + n + 1)) \
                            if tt > CT else 0
                        while s2_done[0] < max(0, q2n - 2):
                            _prep_s2(s, s2_done[0]); s2_done[0] += 1
                _s_ln2(s, NTT - 1)
                while s2_done[0] < NWCH:
                    _prep_s2(s, s2_done[0]); s2_done[0] += 1

        p3 = contextlib.ExitStack()
        sb3, ps3 = mk(p3)
        with p3:
            xqF = [sb3(f"xqF{i}", [128, NJ * TT], BF16) for i in range(2)]
            x2t = sb3("x2t", [128, NC * D], F32)
            g2 = sb3("g2", [128, INNER], BF16)
            t1 = sb3("t1", [128, INNER], F32)
            q2 = [sb3(f"q2_{i}", [128, INNER], BF16) for i in range(2)]
            q2T = [sb3(f"q2T{i}", [128, INNER], BF16) for i in range(2)]
            x3st = [sb3(f"x3st{i}", [128, D], F32) for i in range(2)]
            wg8 = sb3("wg8", [128, NJ * INNER], FP8)
            wu8 = sb3("wu8", [128, NJ * INNER], FP8)
            wd8 = sb3("wd8", [128, NI * D], FP8)
            fcol = sb3("fcol", [128, 32], F32)
            pgu = [ps3(f"pgu{i}", [128, 512]) for i in range(4)]
            pdn = [ps3(f"pdn{i}", [128, 512]) for i in range(2)]
            ptq = [ps3(f"ptq{i}", [128, 512], BF16) for i in range(2)]

            def dpair(k):
                return (pdn[0], pdn[1])

            @blk.gpsimd
            def _(g):
                g.wait_ge(sm_pst, 16 * NWCH)
                g.dma_start(out=wg8[:].rearrange("p (k c) -> p k c", k=NJ),
                            in_=wq_h[0].rearrange("(k p) c -> p k c", p=128)
                            ).then_inc(sm_wq, 16)
                g.dma_start(out=wu8[:].rearrange("p (k c) -> p k c", k=NJ),
                            in_=wq_h[1].rearrange("(k p) c -> p k c", p=128)
                            ).then_inc(sm_wq, 16)
                g.dma_start(out=wd8[:].rearrange("p (k c) -> p k c", k=NI),
                            in_=wq_h[2].rearrange("(k p) c -> p k c", p=128)
                            ).then_inc(sm_wq, 16)

            def _xql(s, tt):
                s.wait_ge(sm_xqt, 16 * (tt * NC + NC))
                if tt >= 2:
                    s.wait_ge(sm_gu, (tt - 2) * NC * 8 + NC * 8)
                for j in range(NJ):
                    s.dma_start(
                        out=xqF[tt % 2][:, j * TT:(j + 1) * TT],
                        in_=xqs_h[tt * TT:(tt + 1) * TT, j * 128:(j + 1) * 128],
                        transpose=True).then_inc(sm_qtl, 16)

            @blk.sync
            def _(s):
                s.wait_ge(sm_pst, 16 * NWCH)
                _xql(s, 0)
                for tt in range(NTT):
                    if tt + 1 < NTT:
                        _xql(s, tt + 1)
                    if tt >= 1:
                        for n in range(NC):
                            k = (tt - 1) * NC + n
                            s.wait_ge(sm_dne, k + 1)
                            s.dma_start(out=out_h[k * 128:(k + 1) * 128, :],
                                        in_=x3st[k % 2][:]).then_inc(sm_ost, 16)
                    s.wait_ge(sm_x2t, 16 * (tt * NC + NC))
                    if tt >= 1:
                        s.wait_ge(sm_dne, tt * NC)
                    s.dma_start(out=x2t[:].rearrange("p (c d) -> p c d", c=NC),
                                in_=x2s_h[tt * TT:(tt + 1) * TT, :].rearrange(
                                    "(c p) d -> p c d", p=128)
                                ).then_inc(sm_x2l, 16)
                for n in range(NC):
                    k = (NTT - 1) * NC + n
                    s.wait_ge(sm_dne, k + 1)
                    s.dma_start(out=out_h[k * 128:(k + 1) * 128, :],
                                in_=x3st[k % 2][:]).then_inc(sm_ost, 16)
                s.wait_ge(sm_ost, 16 * NCH)

            def _transp(t, kq):
                t.wait_ge(sm_q2, kq + 1)
                for b in range(8):
                    if kq * 8 + b >= 2:
                        t.wait_ge(sm_q2e, kq * 8 + b - 1)
                    pb = ptq[b % 2]
                    for ii in range(4):
                        i = b * 4 + ii
                        inst = t.transpose(pb[:, ii * 128:(ii + 1) * 128],
                                           q2[kq % 2][:, i * 128:(i + 1) * 128],
                                           ident[:])
                        if ii == 3:
                            inst.then_inc(sm_q2t, 1)

            def _down(t, kd):
                t.wait_ge(sm_q2e, kd * 8 + 8)
                if kd >= 1:
                    t.wait_ge(sm_dne, kd)
                pair = dpair(kd)
                for i in range(NI):
                    t.matmul(pair[0][:], q2T[kd % 2][:, i * 128:(i + 1) * 128],
                             wd8[:, i * D:i * D + 512],
                             start=(i == 0), stop=(i == NI - 1))
                    inst = t.matmul(pair[1][:],
                                    q2T[kd % 2][:, i * 128:(i + 1) * 128],
                                    wd8[:, i * D + 512:i * D + 1024],
                                    start=(i == 0), stop=(i == NI - 1))
                    if i == NI - 1:
                        inst.then_inc(sm_dn, 1)

            @blk.tensor
            def _(t):
                t.wait_ge(sm_wq, 48)
                t.wait_ge(sm_pv, 2 * NWCH + 2)
                for tt in range(NTT):
                    t.wait_ge(sm_qtl, 16 * (tt * NJ + NJ))
                    for n in range(NC):
                        k = tt * NC + n
                        for p in range(8):
                            ph = k * 8 + p
                            pg, pu = pgu[2 * (p % 2)], pgu[2 * (p % 2) + 1]
                            if ph >= 2:
                                t.wait_ge(sm_sil, ph - 1)
                                t.wait_ge(sm_t1, ph - 1)
                            for d in range(NJ):
                                col = d * INNER + p * 512
                                t.matmul(pg[:],
                                         xqF[tt % 2][:, d * TT + n * 128:
                                                     d * TT + (n + 1) * 128],
                                         wg8[:, col:col + 512],
                                         start=(d == 0), stop=(d == NJ - 1))
                                inst = t.matmul(
                                    pu[:],
                                    xqF[tt % 2][:, d * TT + n * 128:
                                                d * TT + (n + 1) * 128],
                                    wu8[:, col:col + 512],
                                    start=(d == 0), stop=(d == NJ - 1))
                                if d == NJ - 1:
                                    inst.then_inc(sm_gu, 1)
                        if k >= 2:
                            _down(t, k - 2)
                        if k >= 1:
                            _transp(t, k - 1)
                _transp(t, NCH - 1)
                _down(t, NCH - 2)
                _down(t, NCH - 1)

            def _devac(v, kd):
                v.wait_ge(sm_dn, kd + 1)
                v.wait_ge(sm_x2l, 16 * (kd // NC + 1))
                if kd >= 2:
                    v.wait_ge(sm_ost, 16 * (kd - 1))
                pair = dpair(kd)
                c = kd % NC
                fsl = fcol[:, 20 + (kd % 4):21 + (kd % 4)]
                v.scalar_tensor_tensor(out=x3st[kd % 2][:, 0:512], in0=pair[0][:],
                                       scalar=fsl, in1=x2t[:, c * D:c * D + 512],
                                       op0=OP.mult, op1=OP.add)
                v.scalar_tensor_tensor(out=x3st[kd % 2][:, 512:1024],
                                       in0=pair[1][:], scalar=fsl,
                                       in1=x2t[:, c * D + 512:c * D + 1024],
                                       op0=OP.mult, op1=OP.add).then_inc(sm_dne, 1)

            @blk.vector
            def _(v):
                v.memset(fcol[:, 24:32], 1.0)
                for tt in range(NTT):
                    for n in range(NC):
                        k = tt * NC + n
                        for p in range(8):
                            ph = k * 8 + p
                            pu = pgu[2 * (p % 2) + 1]
                            v.wait_ge(sm_gu, ph + 1)
                            v.wait_ge(sm_sil, ph + 1)
                            col = p * 512
                            v.tensor_tensor(out=t1[:, col:col + 512],
                                            in0=pu[:], in1=g2[:, col:col + 512],
                                            op=OP.mult).then_inc(sm_t1, 1)
                        v.tensor_reduce(out=fcol[:, 0:1], in_=t1[:], axis=AX,
                                        op=OP.max, apply_absolute_value=True)
                        if k >= 2:
                            _devac(v, k - 2)
                        v.tensor_copy(out=fcol[:, 28:32], in_=fcol[:, 24:28])
                        v.tensor_copy(out=fcol[:, 24:28], in_=fcol[:, 28:32])
                        v.tensor_tensor(out=fcol[:, 4:5], in0=fcol[:, 0:1],
                                        in1=su_all[:, k:k + 1], op=OP.mult)
                        v.tensor_copy(out=fcol[:, 28:32], in_=fcol[:, 24:28])
                        v.tensor_copy(out=fcol[:, 24:28], in_=fcol[:, 28:32])
                        v.tensor_scalar(out=fcol[:, 8:9], in0=fcol[:, 4:5],
                                        scalar1=1e-5, scalar2=None, op0=OP.max)
                        v.tensor_copy(out=fcol[:, 28:32], in_=fcol[:, 24:28])
                        v.tensor_copy(out=fcol[:, 24:28], in_=fcol[:, 28:32])
                        v.tensor_scalar(out=fcol[:, 20 + (k % 4):21 + (k % 4)],
                                        in0=fcol[:, 8:9],
                                        scalar1=swcols[:, 3:4], scalar2=None,
                                        op0=OP.mult)
                        v.reciprocal(out=fcol[:, 12:13], in_=fcol[:, 8:9])
                        v.tensor_copy(out=fcol[:, 28:32], in_=fcol[:, 24:28])
                        v.tensor_copy(out=fcol[:, 24:28], in_=fcol[:, 28:32])
                        v.tensor_scalar(out=fcol[:, 16:17], in0=fcol[:, 12:13],
                                        scalar1=127.0, scalar2=None, op0=OP.mult)
                        v.tensor_copy(out=fcol[:, 28:32], in_=fcol[:, 24:28])
                        v.tensor_copy(out=fcol[:, 24:28], in_=fcol[:, 28:32])
                        v.tensor_tensor(out=fcol[:, 16:17], in0=fcol[:, 16:17],
                                        in1=su_all[:, k:k + 1], op=OP.mult)
                        v.tensor_copy(out=fcol[:, 28:32], in_=fcol[:, 24:28])
                        v.tensor_copy(out=fcol[:, 24:28], in_=fcol[:, 28:32])
                        if k >= 2:
                            v.wait_ge(sm_q2t, (k - 2) * 8 + 8)
                        _t1b, _fb = bass.broadcast_tensor_aps(t1[:], fcol[:, 16:17])
                        v.scalar_tensor_tensor(out=t1[:], in0=t1[:], scalar=1.0,
                                               in1=_fb, op0=OP.mult, op1=OP.mult)
                        v.tensor_scalar(out=q2[k % 2][:], in0=t1[:],
                                        scalar1=MAGIC, scalar2=-MAGIC,
                                        op0=OP.add, op1=OP.add
                                        ).then_inc(sm_q2, 1)
                _devac(v, NCH - 2)
                _devac(v, NCH - 1)

            def _sevac(s, kq):
                if kq >= 2:
                    s.wait_ge(sm_dn, kq - 1)
                for b in range(8):
                    s.wait_ge(sm_q2t, kq * 8 + b + 1)
                    s.activation(
                        out=q2T[kq % 2].rearrange("p (i t) -> p i t", i=NI)[
                            :, b * 4:b * 4 + 4, 0:128],
                        in_=ptq[b % 2].rearrange("p (i t) -> p i t", i=4)[:],
                        func=AF.Copy).then_inc(sm_q2e, 1)

            @blk.scalar
            def _(s):
                for tt in range(NTT):
                    for n in range(NC):
                        k = tt * NC + n
                        for p in range(8):
                            ph = k * 8 + p
                            pg = pgu[2 * (p % 2)]
                            s.wait_ge(sm_gu, ph + 1)
                            if ph >= 1:
                                s.wait_ge(sm_t1, ph)
                            col = p * 512
                            s.activation(out=g2[:, col:col + 512], in_=pg[:],
                                         func=AF.Silu,
                                         scale=sg_all[:, k:k + 1]
                                         ).then_inc(sm_sil, 1)
                        if k >= 1:
                            _sevac(s, k - 1)
                _sevac(s, NCH - 1)

    return nc


def kernel(L0_attn_x, L0_attn_rms_weight, L0_attn_w_A, L0_attn_b_A, L0_attn_w_x,
           L0_attn_w_sidegate, L0_attn_w_output, L0_attn_w_o, L0_ffn_w_gate,
           L0_ffn_w_up, L0_ffn_w_down):
    x = np.ascontiguousarray(np.asarray(L0_attn_x, dtype=np.float32))
    B, L, _ = x.shape
    nc = build(L=L)
    base = {
        "wA": np.ascontiguousarray(np.asarray(L0_attn_w_A, np.float32)),
        "wSG": np.ascontiguousarray(np.asarray(L0_attn_w_output, np.float32)),
        "wX": np.ascontiguousarray(np.asarray(L0_attn_w_x, np.float32)),
        "wG": np.ascontiguousarray(np.asarray(L0_attn_w_sidegate, np.float32)),
        "wO": np.ascontiguousarray(np.asarray(L0_attn_w_o, np.float32)),
        "rmsT": np.ascontiguousarray(np.broadcast_to(
            np.asarray(L0_attn_rms_weight, np.float32), (128, D))),
        "bAc": np.ascontiguousarray(
            np.asarray(L0_attn_b_A, np.float32).reshape(NJ, 128).T),
        "wgT": np.ascontiguousarray(np.asarray(L0_ffn_w_gate, np.float32).T),
        "wuT": np.ascontiguousarray(np.asarray(L0_ffn_w_up, np.float32).T),
        "wdT": np.ascontiguousarray(np.asarray(L0_ffn_w_down, np.float32).T),
        "idb": np.eye(128, dtype=ml_dtypes.bfloat16),
        "ones1": np.ones((1, 128), np.float32),
        "onesc": np.ones((128, 1), np.float32),
    }
    in_maps = [dict(base, x=np.ascontiguousarray(x[c])) for c in range(B)]
    res = run_bass_kernel_spmd(nc, in_maps, core_ids=list(range(B)))
    return np.stack([np.asarray(res.results[c]["out"]) for c in range(B)])
